# version 34 (final)
# baseline (speedup 1.0000x reference)
import numpy as np
import concourse.bass as bass
import concourse.mybir as mybir
from concourse.bass_utils import run_bass_kernel_spmd
from contextlib import ExitStack

F32 = mybir.dt.float32
BF16 = mybir.dt.bfloat16
F16 = mybir.dt.float16
AF = mybir.ActivationFunctionType
ALU = mybir.AluOpType
AX = mybir.AxisListType

P = 128
NT = 16
D = 1024
NC8 = 8
NH = 16
HD = 64
NPAIR = 8
FF = 4096
NSUP = 4
EPS = 1e-6
LN_EPS = 1e-5

THETA = (0.24419428096993973, 0.4639691338428854, 0.6861859295255306,
         0.9059607823984763, 1.0839784132255623)

WQ_OFF = 0
WO_OFF = 16384
WK_OFF = 32768
WV_OFF = 49152
XBUF_OFF = 65536
SQ_OFF = 73728
XN_OFF = 81920
EXPA_OFF = 86016
KSA_OFF = 90112
VSA_OFF = 94208
YA_OFF = 65536
HT_OFF = 98304
EXPB_OFF = 98304
KSB_OFF = 102400
VSB_OFF = 106496
CTB_OFF = 110592
SG_OFFS = {0: (114688, 116736, 118784, 120832, 122880),
           1: (110592, 112640, 126976, 129024, 131072)}
FB_OFFS = {0: 124928, 1: 133120}
POOL_BYTES = 135168
POOL_F32 = POOL_BYTES // 4

SLOT_Q, SLOT_K, SLOT_V, SLOT_O = 0, 1, 2, 3
WA_OFFS = {SLOT_Q: WQ_OFF, SLOT_K: WK_OFF, SLOT_V: WV_OFF, SLOT_O: WO_OFF}

ST_SUM, ST_SQ, ST_MEAN, ST_EM2, ST_VEPS, ST_NMR, ST_T0, ST_T1 = range(8)
ST_NMAX, ST_S16, ST_CK = 8, 24, 40


def build_nc():
    nc = bass.Bass("TRN2", target_bir_lowering=False,
                   detect_race_conditions=False)

    x_hbm = nc.dram_tensor("x", [2048, D], F32, kind="ExternalInput")
    w_hbm = {
        SLOT_Q: nc.dram_tensor("wq", [D, D], F32, kind="ExternalInput"),
        SLOT_K: nc.dram_tensor("wk", [D, D], F32, kind="ExternalInput"),
        SLOT_V: nc.dram_tensor("wv", [D, D], F32, kind="ExternalInput"),
        SLOT_O: nc.dram_tensor("wo", [D, D], F32, kind="ExternalInput"),
    }
    b_hbm = {
        SLOT_Q: nc.dram_tensor("bq", [1, D], F32, kind="ExternalInput"),
        SLOT_K: nc.dram_tensor("bk", [1, D], F32, kind="ExternalInput"),
        SLOT_V: nc.dram_tensor("bv", [1, D], F32, kind="ExternalInput"),
        SLOT_O: nc.dram_tensor("bo", [1, D], F32, kind="ExternalInput"),
    }
    wfc_hbm = nc.dram_tensor("wfc", [D, FF], F32, kind="ExternalInput")
    wpr_hbm = nc.dram_tensor("wpr", [FF, D], F32, kind="ExternalInput")
    out_hbm = nc.dram_tensor("out", [2048, D], F32, kind="ExternalOutput")

    with ExitStack() as es:
        pool = es.enter_context(nc.sbuf_tensor("pool", [P, POOL_F32], F32))
        out_acc = es.enter_context(nc.sbuf_tensor("oacc", [P, NT, D], BF16))
        xnT = es.enter_context(nc.sbuf_tensor("xnT", [P, NC8, 2048], BF16))
        kvbd = es.enter_context(nc.sbuf_tensor("kvbd", [P, D], BF16))
        ident = es.enter_context(nc.sbuf_tensor("ident", [P, P], BF16))
        brows = es.enter_context(nc.sbuf_tensor("brows", [1, 4224], BF16))
        st = es.enter_context(nc.sbuf_tensor("st", [P, 2, 64], F32))
        cst = es.enter_context(nc.sbuf_tensor("cst", [P, 8], F32))
        dscr = None
        ppool = es.enter_context(nc.psum_tensor("ppool", [P, 4, 1024], F32))

        pool_b16 = pool[:].bitcast(BF16)
        pool_f16 = pool[:].bitcast(F16)
        pool_f32 = pool[:]

        def bfv(off, n):
            return pool_b16[:, off // 2: off // 2 + n]

        def f16v(off, n):
            return pool_f16[:, off // 2: off // 2 + n]

        def f32v(off, n):
            return pool_f32[:, off // 4: off // 4 + n]

        def wa(m):
            return bfv(WA_OFFS[m], NC8 * 1024).rearrange(
                "p (c n) -> p c n", n=1024)

        wf_v = bfv(WK_OFF, NC8 * 1024).rearrange("p (c n) -> p c n", n=1024)
        wp_v = bfv(WV_OFF, NC8 * 1024).rearrange("p (c n) -> p c n", n=1024)
        ht_v = bfv(HT_OFF, NC8 * 1024).rearrange("p (c n) -> p c n", n=1024)
        ht_f16 = f16v(HT_OFF, NC8 * 1024).rearrange("p (c n) -> p c n",
                                                    n=1024)
        ya_v = f16v(YA_OFF, NT * 1024).rearrange("p (c n) -> p c n", n=1024)

        def xbuf(sl):
            return f32v(XBUF_OFF + sl * 4096, 1024)

        def sqb(sl):
            return f32v(SQ_OFF + sl * 4096, 1024)

        def xnb(sl):
            return bfv(XN_OFF + sl * 2048, 1024)

        def expa(sl):
            return f16v(EXPA_OFF + sl * 2048, 1024)

        def ksa(sl):
            return bfv(KSA_OFF + sl * 2048, 1024)

        def vsa(sl):
            return bfv(VSA_OFF + sl * 2048, 1024)

        def expb(sl):
            return f16v(EXPB_OFF + sl * 2048, 1024)

        def ksb(sl):
            return bfv(KSB_OFF + sl * 2048, 1024)

        def vsb(sl):
            return bfv(VSB_OFF + sl * 2048, 1024)

        def ctb(sl):
            return bfv(CTB_OFF + sl * 2048, 1024)

        def sgv(ctx, j):
            return f16v(SG_OFFS[ctx][j], 1024)

        def fbv(ctx):
            return f16v(FB_OFFS[ctx], 1024)

        R = [ppool[:, i, :] for i in range(4)]
        PT = ppool[:].bitcast(BF16)[:, 3, 1024:2048]
        PT3 = PT.rearrange("p (c n) -> p c n", n=P)
        PTs = (ppool[:].bitcast(BF16)[:, 3, 0:1024], PT)

        def r3d(i):
            return ppool[:, i, :].rearrange("p (h d) -> p h d", d=HD)

        ONES = brows[0:1, 4096:4224]

        def brow(m):
            return brows[0:1, m * 1024:(m + 1) * 1024]

        C_05 = cst[:, 0:1]
        C_EPSA = cst[:, 6:7]
        C_EPS2 = cst[:, 7:8]

        def c_nth(j):
            return cst[:, 1 + j:2 + j]

        sem_names = (["T", "V", "S", "G", "DX", "DO", "DWB", "DWK", "DWV",
                      "DWQ", "DWO"]
                     + [f"DF{s}" for s in range(NSUP)]
                     + [f"DP{s}" for s in range(NSUP)])
        sems = {n: es.enter_context(nc.semaphore(n)) for n in sem_names}

        prog = {"sync": [], "tensor": [], "vector": [], "scalar": [],
                "gpsimd": []}
        cnt = {n: 0 for n in sem_names}
        lastw = {}

        def emit(eng, fn):
            prog[eng].append(fn)

        def inc(eng, sem, k, fn, k_total=None):
            cnt[sem] += k if k_total is None else k_total
            tok = cnt[sem]
            prog[eng].append(
                lambda e, fn=fn, sem=sem, k=k: fn(e).then_inc(sems[sem], k))
            return tok

        def wait(eng, sem, val):
            if val is None or val <= 0:
                return
            key = (eng, sem)
            if lastw.get(key, 0) >= val:
                return
            lastw[key] = val
            prog[eng].append(
                lambda e, sem=sem, val=val: e.wait_ge(sems[sem], val))

        def stv(t, col, n=1):
            return st[:, t & 1, col:col + n]

        emit("gpsimd", lambda g: g.memset(ident[:], 0.0))
        emit("gpsimd", lambda g: g.memset(kvbd[:], 0.0))
        emit("gpsimd", lambda g: g.memset(C_05, 0.5))
        emit("gpsimd", lambda g: g.memset(C_EPSA, EPS))
        emit("gpsimd", lambda g: g.memset(C_EPS2, EPS * EPS))
        for j in range(5):
            emit("gpsimd", lambda g, j=j: g.memset(c_nth(j), -THETA[j]))
        emit("gpsimd", lambda g: g.memset(st[:, 0, 56:58], 0.0))
        emit("gpsimd", lambda g: g.drain())
        emit("gpsimd", lambda g: g.affine_select(
            out=ident[:], in_=ident[:], compare_op=ALU.not_equal, fill=1.0,
            base=0, pattern=[[-1, P]], channel_multiplier=1))
        g_setup = inc("gpsimd", "G", 1, lambda g: g.memset(ONES, 1.0))

        DWB_TOK = {}
        for m, semn in ((SLOT_K, "DWK"), (SLOT_V, "DWV"), (SLOT_Q, "DWQ"),
                        (SLOT_O, "DWO")):
            for c in range(NC8):
                inc("gpsimd", semn, 16,
                    lambda g, m=m, c=c: g.dma_start(
                        out=wa(m)[:, c, :],
                        in_=w_hbm[m][c * P:(c + 1) * P, :]))
            inc("gpsimd", semn, 16,
                lambda g, m=m: g.dma_start(
                    out=f32v(133120 + m * 512, 128)[0:1, :],
                    in_=b_hbm[m][:, 0:128]))
            DWB_TOK[m] = inc("gpsimd", "DWB", 16,
                             lambda g, m=m: g.dma_start(out=brow(m),
                                                        in_=b_hbm[m][:, :]))
        tok_dwb = cnt["DWB"]

        tok_x = [0] * NT
        tok_sq = [0] * NT
        tok_oinit = [0] * NT
        tok_sqrd = [0] * NT
        tok_xn = [0] * NT
        tok_tp = [0] * NT
        tok_ptfree = [0] * NT
        tok_kproj = [0] * NT
        tok_r0free = [0] * NT
        tok_vproj = [0] * NT
        tok_expk = [0] * NT
        tok_s16 = [0] * NT
        tok_ck = [0] * NT
        tok_ks = [0] * NT
        tok_vs = [0] * NT
        tok_kv = [0] * NT

        def tg(lst, t):
            return lst[t] if t >= 0 else 0

        def emit_tp(t):
            wait("tensor", "S", tok_xn[t])
            wait("tensor", "G", g_setup)
            if t >= 1:
                wait("tensor", "S", tok_ptfree[t - 1])
            for c in range(NC8):
                fn = lambda e, t=t, c=c: e.transpose(
                    PT[:, c * P:(c + 1) * P],
                    xnb(t & 1)[:, c * P:(c + 1) * P], ident[:])
                if c < NC8 - 1:
                    emit("tensor", fn)
                else:
                    tok_tp[t] = inc("tensor", "T", 1, fn)

        JS2 = (slice(0, 512), slice(512, 1024))

        WSEM = {SLOT_K: "DWK", SLOT_V: "DWV", SLOT_Q: "DWQ",
                SLOT_O: "DWO"}
        WOFF = {SLOT_K: 0, SLOT_V: 0, SLOT_Q: 0, SLOT_O: 0}

        def emit_projmm(t, slot, rdst, tok_list):
            for c in range(NC8):
                wait("tensor", WSEM[slot], 16 * (c + 2))
                for js in JS2:
                    emit("tensor", lambda e, t=t, c=c, js=js, slot=slot,
                         rdst=rdst: e.matmul(
                             R[rdst][:, js], xnT[:, c, t * P:(t + 1) * P],
                             wa(slot)[:, c, js], start=(c == 0), stop=False))
            wait("tensor", "DWB", DWB_TOK[slot])
            for ji, js in enumerate(JS2):
                fn = lambda e, js=js, slot=slot, rdst=rdst: e.matmul(
                    R[rdst][:, js], ONES, brow(slot)[:, js], start=False,
                    stop=True)
                if ji == 1:
                    tok_list[t] = inc("tensor", "T", 1, fn)
                else:
                    emit("tensor", fn)

        def emit_kv(t):
            wait("tensor", "V", tok_ks[t])
            wait("tensor", "V", tok_vs[t])
            for p_ in range(NPAIR):
                ps = slice(p_ * P, (p_ + 1) * P)
                fn = lambda e, t=t, ps=ps, p_=p_: e.matmul(
                    R[2][:, ps], ksa(t & 1)[:, ps], vsa(t & 1)[:, ps],
                    start=(t == 0 and p_ % 4 == 0), stop=(t == NT - 1),
                    skip_group_check=True)
                if p_ == NPAIR - 1:
                    tok_kv[t] = inc("tensor", "T", 1, fn)
                else:
                    emit("tensor", fn)

        def emit_rotA(t):
            if t >= 2:
                wait("vector", "S", tok_ck[t - 2])
                if t >= 4:
                    wait("vector", "T", tok_kv[t - 4])
                ck2 = stv(t - 2, ST_CK, NH).rearrange(
                    "p (h o) -> p h o", o=1).broadcast_to([P, NH, HD])
                tok_ks[t - 2] = inc(
                    "vector", "V", 1, lambda v, t=t, ck2=ck2: v.tensor_tensor(
                        ksa(t & 1).rearrange("p (h d) -> p h d", d=HD),
                        expa(t & 1).rearrange("p (h d) -> p h d", d=HD),
                        ck2, ALU.mult))
            wait("vector", "T", tok_kproj[t])
            emit("vector", lambda v, t=t: v.tensor_reduce(
                stv(t, ST_NMAX, NH), r3d(0), AX.X, ALU.max, negate=True))
            if t >= 1:
                wait("vector", "S", tok_expk[t - 1])
                tok_s16[t - 1] = inc("vector", "V", 1, lambda v, t=t:
                                     v.tensor_reduce(
                                         stv(t - 1, ST_S16, NH),
                                         expa((t - 1) & 1).rearrange(
                                             "p (h d) -> p h d", d=HD),
                                         AX.X, ALU.add))
            nmb = stv(t, ST_NMAX, NH).rearrange(
                "p (h o) -> p h o", o=1).broadcast_to([P, NH, HD])
            tok_r0free[t] = inc(
                "vector", "V", 1, lambda v, t=t, nmb=nmb: v.tensor_tensor(
                    expa(t & 1).rearrange("p (h d) -> p h d", d=HD), r3d(0),
                    nmb, ALU.add))

        def emit_vsA(t):
            wait("vector", "T", tok_vproj[t])
            if t >= 2:
                assert tok_kv[t - 2] > 0
                wait("vector", "T", tok_kv[t - 2])
            tok_vs[t] = inc("vector", "V", 1,
                            lambda v, t=t: v.tensor_copy(vsa(t & 1), R[1]))

        def emit_sblkA(t):
            wait("scalar", "V", tok_r0free[t])
            tok_expk[t] = inc("scalar", "S", 1, lambda s, t=t: s.activation(
                expa(t & 1), expa(t & 1), AF.Exp))
            if t >= 1:
                wait("scalar", "V", tok_s16[t - 1])
                emit("scalar", lambda s, t=t: s.activation(
                    stv(t - 1, ST_CK, NH), stv(t - 1, ST_S16, NH), AF.Ln,
                    bias=C_EPSA))
                emit("scalar", lambda s: s.copy(
                    st[:, 0, 56:57], st[:, 0, 57:58]))
                tok_ck[t - 1] = inc("scalar", "S", 1, lambda s, t=t:
                                    s.activation(
                                        stv(t - 1, ST_CK, NH),
                                        stv(t - 1, ST_CK, NH), AF.Exp,
                                        scale=-1.0))

        def emit_headA(pt):
            if pt >= NT:
                return
            a, b = pt, pt + 1
            for t in (a, b):
                if t >= 2:
                    wait("sync", "S", tok_xn[t - 2])
                    wait("sync", "V", tok_sqrd[t - 2])
                    wait("sync", "S", tok_oinit[t - 2])
                if t >= 1:
                    wait("sync", "DX", tok_x[t - 1])
                tok_x[t] = inc("sync", "DX", 16,
                               lambda e, t=t: e.dma_start(
                                   out=xbuf(t & 1),
                                   in_=x_hbm[t * P:(t + 1) * P, :]))
            for t in (a, b):
                wait("scalar", "DX", tok_x[t])
                if t >= 2:
                    wait("scalar", "V", tok_sqrd[t - 2])
            tok_sq[a] = inc("scalar", "S", 1,
                            lambda s, a=a: s.square(sqb(a & 1), xbuf(a & 1)))
            tok_sq[b] = inc("scalar", "S", 1,
                            lambda s, b=b: s.square(sqb(b & 1), xbuf(b & 1)))
            for t in (a, b):
                tok_oinit[t] = inc(
                    "scalar", "S", 1,
                    lambda s, t=t: s.copy(out_acc[:, t, :], xbuf(t & 1)))
            for t in (a, b):
                wait("vector", "DX", tok_x[t])
                emit("vector", lambda v, t=t: v.tensor_reduce(
                    stv(t, ST_SUM), xbuf(t & 1), AX.X, ALU.add))
            for t in (a, b):
                wait("vector", "S", tok_sq[t])
                tok_sqrd[t] = inc("vector", "V", 1, lambda v, t=t:
                                  v.tensor_reduce(stv(t, ST_SQ), sqb(t & 1),
                                                  AX.X, ALU.add))
            for t in (a, b):
                emit("vector", lambda v, t=t: v.tensor_scalar(
                    stv(t, ST_MEAN), stv(t, ST_SUM), -1.0 / D, None,
                    ALU.mult))
            for t in (a, b):
                emit("vector", lambda v, t=t: v.tensor_scalar(
                    stv(t, ST_EM2), stv(t, ST_SQ), 1.0 / D, None, ALU.mult))
            for t in (a, b):
                emit("vector", lambda v, t=t: v.tensor_scalar(
                    stv(t, ST_T0), stv(t, ST_MEAN), stv(t, ST_MEAN), -1.0,
                    ALU.mult, ALU.mult))
            tok_veps = {}
            for t in (a, b):
                tok_veps[t] = inc("vector", "V", 1, lambda v, t=t:
                                  v.tensor_scalar(
                                      stv(t, ST_VEPS), stv(t, ST_EM2),
                                      stv(t, ST_T0), LN_EPS, ALU.add,
                                      ALU.add))
            for t in (a, b):
                wait("scalar", "V", tok_veps[t])
                emit("scalar", lambda s, t=t: s.activation(
                    stv(t, ST_T1), stv(t, ST_VEPS), AF.Ln))
            for t in (a, b):
                emit("scalar", lambda s, t=t: s.activation(
                    stv(t, ST_VEPS), stv(t, ST_T1), AF.Exp, scale=-0.5))
            for t in (a, b):
                emit("scalar", lambda s, t=t: s.activation(
                    stv(t, ST_NMR), stv(t, ST_MEAN), AF.Identity,
                    scale=stv(t, ST_VEPS)))
            for t in (a, b):
                if t >= 2:
                    wait("scalar", "T", tok_tp[t - 2])
                tok_xn[t] = inc("scalar", "S", 1, lambda s, t=t:
                                s.activation(xnb(t & 1), xbuf(t & 1),
                                             AF.Identity,
                                             bias=stv(t, ST_NMR),
                                             scale=stv(t, ST_VEPS)))

        emit_headA(0)
        for pt in range(0, NT, 2):
            a, b = pt, pt + 1

            emit_tp(a)
            wait("scalar", "T", tok_tp[a])
            tok_ptfree[a] = inc(
                "scalar", "S", 1,
                lambda s, a=a: s.copy(xnT[:, :, a * P:(a + 1) * P], PT3))
            if pt == 0:
                emit_tp(b)
                wait("scalar", "T", tok_tp[b])
                tok_ptfree[b] = inc(
                    "scalar", "S", 1,
                    lambda s, b=b: s.copy(xnT[:, :, b * P:(b + 1) * P], PT3))
            wait("tensor", "S", tok_ptfree[a])
            wait("tensor", "V", tg(tok_r0free, a - 1))
            emit_projmm(a, SLOT_K, 0, tok_kproj)
            if pt != 0:
                emit_tp(b)
            wait("tensor", "V", tg(tok_vs, a - 1))
            emit_projmm(a, SLOT_V, 1, tok_vproj)
            if pt != 0:
                wait("scalar", "T", tok_tp[b])
                tok_ptfree[b] = inc(
                    "scalar", "S", 1,
                    lambda s, b=b: s.copy(xnT[:, :, b * P:(b + 1) * P], PT3))
            emit_rotA(a)
            if pt >= 2:
                emit_kv(a - 2)
            emit_vsA(a)
            emit_sblkA(a)
            emit_headA(pt + 2)
            wait("tensor", "S", tok_ptfree[b])
            wait("tensor", "V", tok_r0free[a])
            emit_projmm(b, SLOT_K, 0, tok_kproj)
            wait("tensor", "V", tok_vs[a])
            emit_projmm(b, SLOT_V, 1, tok_vproj)
            emit_rotA(b)
            if pt >= 2:
                emit_kv(b - 2)
            emit_vsA(b)
            emit_sblkA(b)

        wait("vector", "S", tok_expk[NT - 1])
        tok_s16[NT - 1] = inc("vector", "V", 1, lambda v: v.tensor_reduce(
            stv(NT - 1, ST_S16, NH),
            expa((NT - 1) & 1).rearrange("p (h d) -> p h d", d=HD),
            AX.X, ALU.add))
        wait("vector", "S", tok_ck[NT - 2])
        ck14 = stv(NT - 2, ST_CK, NH).rearrange(
            "p (h o) -> p h o", o=1).broadcast_to([P, NH, HD])
        tok_ks[NT - 2] = inc("vector", "V", 1, lambda v, ck14=ck14:
                             v.tensor_tensor(
                                 ksa(0).rearrange("p (h d) -> p h d", d=HD),
                                 expa(0).rearrange("p (h d) -> p h d", d=HD),
                                 ck14, ALU.mult))
        wait("scalar", "V", tok_s16[NT - 1])
        emit("scalar", lambda s: s.activation(
            stv(NT - 1, ST_CK, NH), stv(NT - 1, ST_S16, NH), AF.Ln,
            bias=C_EPSA))
        emit("scalar", lambda s: s.drain())
        tok_ck[NT - 1] = inc("scalar", "S", 1, lambda s: s.activation(
            stv(NT - 1, ST_CK, NH), stv(NT - 1, ST_CK, NH), AF.Exp,
            scale=-1.0))
        wait("vector", "S", tok_ck[NT - 1])
        ck15 = stv(NT - 1, ST_CK, NH).rearrange(
            "p (h o) -> p h o", o=1).broadcast_to([P, NH, HD])
        tok_ks[NT - 1] = inc("vector", "V", 1, lambda v, ck15=ck15:
                             v.tensor_tensor(
                                 ksa(1).rearrange("p (h d) -> p h d", d=HD),
                                 expa(1).rearrange("p (h d) -> p h d", d=HD),
                                 ck15, ALU.mult))
        emit_kv(NT - 2)
        emit_kv(NT - 1)

        wait("scalar", "T", tok_kv[NT - 1])
        tok_kvbd = None
        for p_ in range(NPAIR):
            c0 = p_ * P
            emit("scalar", lambda s, c0=c0: s.copy(
                kvbd[0:HD, c0:c0 + HD], R[2][0:HD, c0:c0 + HD]))
            fn = lambda s, c0=c0: s.copy(
                kvbd[HD:P, c0 + HD:c0 + P], R[2][HD:P, c0 + HD:c0 + P])
            if p_ == NPAIR - 1:
                tok_kvbd = inc("scalar", "S", 1, fn)
            else:
                emit("scalar", fn)

        wait("tensor", "S", tok_kvbd)
        wait("tensor", "S", tok_ptfree[NT - 1])
        wait("tensor", "DWO", 16 * NC8)
        for p_ in range(NPAIR):
            fn = lambda e, p_=p_: e.transpose(
                PT[:, p_ * P:(p_ + 1) * P], kvbd[:, p_ * P:(p_ + 1) * P],
                ident[:])
            if p_ == NPAIR - 1:
                tok_mtp = inc("tensor", "T", 1, fn)
            else:
                emit("tensor", fn)
        wait("scalar", "T", tok_mtp)
        tok_kvT = inc("scalar", "S", 1, lambda s: s.copy(ksa(0), PT))
        wait("tensor", "S", tok_kvT)
        wait("tensor", "V", tok_r0free[NT - 1])
        wait("tensor", "V", tok_vs[NT - 1])
        tok_m_ev = {0: None, 1: None}
        for p_ in range(NPAIR):
            rm = p_ & 1
            if tok_m_ev[rm] is not None:
                wait("tensor", "S", tok_m_ev[rm])
            last = None
            for jh in range(2):
                js = slice(jh * 512, (jh + 1) * 512)
                fn = lambda e, p_=p_, js=js, rm=rm: e.matmul(
                    R[rm][:, js], ksa(0)[:, p_ * P:(p_ + 1) * P],
                    wa(SLOT_O)[:, p_, js], start=True, stop=True,
                    skip_group_check=True)
                if jh == 1:
                    last = inc("tensor", "T", 1, fn)
                else:
                    emit("tensor", fn)
            wait("scalar", "T", last)
            tok_m_ev[rm] = inc("scalar", "S", 1,
                               lambda s, p_=p_, rm=rm: s.copy(
                                   wa(SLOT_O)[:, p_, :], R[rm]))

        tok_df = [0] * NSUP
        tok_dp = [0] * NSUP
        tok_fc_done = [0] * NSUP
        tok_pj_done = [0] * NSUP

        def emit_wf_dma(s):
            wait("gpsimd", "T",
                 tok_kproj[NT - 1] if s == 0 else tok_fc_done[s - 1])
            src = wfc_hbm[:, s * 1024:(s + 1) * 1024].rearrange(
                "(c p) n -> p c n", p=P)
            inc("gpsimd", f"DF{s}", 16,
                lambda g, src=src: g.dma_start(out=wf_v, in_=src))
            tok_df[s] = cnt[f"DF{s}"]

        def emit_wp_dma(s):
            wait("gpsimd", "T",
                 tok_vproj[NT - 1] if s == 0 else tok_pj_done[s - 1])
            src = wpr_hbm[s * 1024:(s + 1) * 1024, :].rearrange(
                "(c p) n -> p c n", p=P)
            inc("gpsimd", f"DP{s}", 16,
                lambda g, src=src: g.dma_start(out=wp_v, in_=src))
            tok_dp[s] = cnt[f"DP{s}"]

        emit_wf_dma(0)
        emit_wp_dma(0)

        fc_ctr = 0
        tok_pa_free = {0: tok_m_ev[0], 1: tok_m_ev[1]}
        tok_pa_kind = {0: "S", 1: "S"}
        tok_py_free = {2: tok_kvbd, 3: tok_kvT}
        tok_py_kind = {2: "S", 3: "S"}
        tok_y_ready = [0] * NT
        tok_pj_done_prev = 0
        tok_ht_last = None

        for s in range(NSUP):
            tok_relu_f = [0] * NC8
            for H in range(2):
                prev_sq = None
                for fcc in range(NC8):
                    pa = fc_ctr & 1
                    fc_ctr += 1
                    wait("tensor", f"DF{s}", tok_df[s])
                    wait("tensor", tok_pa_kind[pa], tok_pa_free[pa])
                    last = None
                    for c in range(NC8):
                        for jh in range(2):
                            js = slice(jh * 512, (jh + 1) * 512)
                            fn = (lambda e, pa=pa, c=c, fcc=fcc, H=H, js=js,
                                  jh=jh:
                                  e.matmul(R[pa][:, js],
                                           wf_v[:, c, fcc * P:(fcc + 1) * P],
                                           xnT[:, c, H * 1024 + jh * 512:
                                               H * 1024 + (jh + 1) * 512],
                                           start=(c == 0),
                                           stop=(c == NC8 - 1)))
                            if jh == 1 and c == NC8 - 1:
                                last = inc("tensor", "T", 1, fn)
                            else:
                                emit("tensor", fn)
                    if H == 1 and fcc == NC8 - 1:
                        tok_fc_done[s] = last

                    wait("scalar", "T", last)
                    wait("scalar", "T", tok_pj_done_prev)
                    tok_pa_free[pa] = inc(
                        "scalar", "S", 1,
                        lambda sc, pa=pa, fcc=fcc: sc.activation(
                            ht_f16[:, fcc, :], R[pa], AF.Relu))
                    tok_pa_kind[pa] = "S"
                    tok_relu_f[fcc] = tok_pa_free[pa]
                    if prev_sq is not None:
                        ppa, pfcc = prev_sq
                        wait("vector", "S", tok_relu_f[pfcc])
                        emit("vector", lambda v, pfcc=pfcc: v.tensor_tensor(
                            ht_v[:, pfcc, :], ht_f16[:, pfcc, :],
                            ht_f16[:, pfcc, :], ALU.mult))
                    prev_sq = (pa, fcc)
                ppa, pfcc = prev_sq
                wait("vector", "S", tok_relu_f[pfcc])
                tok_ht_ready = inc(
                    "vector", "V", 1,
                    lambda v, pfcc=pfcc: v.tensor_tensor(
                        ht_v[:, pfcc, :], ht_f16[:, pfcc, :],
                        ht_f16[:, pfcc, :], ALU.mult))
                tok_ht_last = tok_ht_ready

                for ts_ in range(8):
                    tglob = H * 8 + ts_
                    py = 2 + (ts_ & 1)
                    wait("tensor", f"DP{s}", tok_dp[s])
                    wait("tensor", "V", tok_ht_ready)
                    wait("tensor", tok_py_kind[py], tok_py_free[py])
                    last = None
                    for fcc in range(NC8):
                        for jh in range(2):
                            js = slice(jh * 512, (jh + 1) * 512)
                            fn = (lambda e, py=py, fcc=fcc, js=js, ts_=ts_:
                                  e.matmul(
                                      R[py][:, js],
                                      ht_v[:, fcc, ts_ * P:(ts_ + 1) * P],
                                      wp_v[:, fcc, js],
                                      start=(fcc == 0), stop=(fcc == NC8 - 1)))
                            if fcc == NC8 - 1 and jh == 1:
                                last = inc("tensor", "T", 1, fn)
                            else:
                                emit("tensor", fn)
                    if H == 1 and ts_ == 7:
                        tok_pj_done[s] = last

                    if s == 0:
                        wait("scalar", "T", last)
                        wait("scalar", "V", tok_ks[NT - 1])
                        tok_py_free[py] = inc(
                            "scalar", "S", 1,
                            lambda sc, py=py, tglob=tglob: sc.copy(
                                ya_v[:, tglob, :], R[py]))
                        tok_py_kind[py] = "S"
                    else:
                        wait("vector", "T", last)
                        tok_py_free[py] = inc(
                            "vector", "V", 1,
                            lambda v, py=py, tglob=tglob: v.tensor_tensor(
                                ya_v[:, tglob, :], R[py], ya_v[:, tglob, :],
                                ALU.add))
                        tok_py_kind[py] = "V"
                        if s == NSUP - 1:
                            tok_y_ready[tglob] = tok_py_free[py]
                tok_pj_done_prev = last
            if s + 1 < NSUP:
                emit_wf_dma(s + 1)
                emit_wp_dma(s + 1)

        tok_qproj = [0] * NT
        tok_r0b = [0] * NT
        tok_expq = [0] * NT
        tok_s16b = [0] * NT
        tok_cq = [0] * NT
        tok_qs = [0] * NT
        tok_qtp = [0] * NT
        tok_qst = [0] * NT
        tok_ctx = [0] * NT
        tok_ctxe = [0] * NT
        tok_attn = [0] * NT
        tok_oadd = [0] * NT
        tok_a3 = [0] * NT
        tok_asum = [0] * NT
        tok_fb = [0] * NT
        tok_do2 = [0] * NT

        def emit_qproj(t):
            if t == 0:
                wait("tensor", tok_pa_kind[0], tok_pa_free[0])
                wait("tensor", tok_pa_kind[1], tok_pa_free[1])
            else:
                wait("tensor", "V", tok_r0b[t - 1])
            emit_projmm(t, SLOT_Q, 0, tok_qproj)

        def emit_rotB(t):
            if t >= 2:
                wait("vector", "S", tok_cq[t - 2])
                if t >= 4:
                    wait("vector", "T", tok_qtp[t - 4])
                cq2 = stv(t - 2, ST_CK, NH).rearrange(
                    "p (h o) -> p h o", o=1).broadcast_to([P, NH, HD])
                tok_qs[t - 2] = inc(
                    "vector", "V", 1, lambda v, t=t, cq2=cq2: v.tensor_tensor(
                        ksb(t & 1).rearrange("p (h d) -> p h d", d=HD),
                        expb(t & 1).rearrange("p (h d) -> p h d", d=HD),
                        cq2, ALU.mult))
            wait("vector", "T", tok_qproj[t])
            emit("vector", lambda v, t=t: v.tensor_reduce(
                stv(t, ST_NMAX, NH), r3d(0), AX.X, ALU.max, negate=True))
            if t >= 1:
                wait("vector", "S", tok_expq[t - 1])
                tok_s16b[t - 1] = inc("vector", "V", 1, lambda v, t=t:
                                      v.tensor_reduce(
                                          stv(t - 1, ST_S16, NH),
                                          expb((t - 1) & 1).rearrange(
                                              "p (h d) -> p h d", d=HD),
                                          AX.X, ALU.add))
            nmb = stv(t, ST_NMAX, NH).rearrange(
                "p (h o) -> p h o", o=1).broadcast_to([P, NH, HD])
            tok_r0b[t] = inc(
                "vector", "V", 1, lambda v, t=t, nmb=nmb: v.tensor_tensor(
                    expb(t & 1).rearrange("p (h d) -> p h d", d=HD), r3d(0),
                    nmb, ALU.add))

        def emit_sblkB(t):
            wait("scalar", "V", tok_r0b[t])
            tok_expq[t] = inc("scalar", "S", 1, lambda s, t=t: s.activation(
                expb(t & 1), expb(t & 1), AF.Exp))
            if t >= 1:
                wait("scalar", "V", tok_s16b[t - 1])
                emit("scalar", lambda s, t=t: s.activation(
                    stv(t - 1, ST_CK, NH), stv(t - 1, ST_S16, NH), AF.Ln,
                    scale=1.0 + EPS, bias=C_EPS2))
                emit("scalar", lambda s: s.copy(
                    st[:, 0, 56:57], st[:, 0, 57:58]))
                tok_cq[t - 1] = inc("scalar", "S", 1, lambda s, t=t:
                                    s.activation(
                                        stv(t - 1, ST_CK, NH),
                                        stv(t - 1, ST_CK, NH), AF.Exp,
                                        scale=-1.0))

        def emit_qtp(t2):
            wait("tensor", "V", tok_qs[t2])
            if t2 < 2:
                wait("tensor", "V", tok_y_ready[NT - 1])
            else:
                wait("tensor", "V", tok_qst[t2 - 2])
            for c in range(NC8):
                fn = lambda e, t=t2, c=c: e.transpose(
                    PTs[t & 1][:, c * P:(c + 1) * P],
                    ksb(t & 1)[:, c * P:(c + 1) * P], ident[:])
                if c < NC8 - 1:
                    emit("tensor", fn)
                else:
                    tok_qtp[t2] = inc("tensor", "T", 1, fn)

        def emit_qst(t2):
            wait("vector", "T", tok_qtp[t2])
            if t2 >= 2:
                wait("vector", "T", tok_attn[t2 - 2])
            tok_qst[t2] = inc("vector", "V", 1,
                              lambda v, t=t2: v.tensor_copy(vsb(t & 1),
                                                            PTs[t & 1]))

        def emit_attn(t2):
            rb = R[2] if t2 % 2 == 0 else R[1]
            wait("tensor", "V", tok_qst[t2])
            if t2 == 0:
                wait("tensor", "V", tok_y_ready[NT - 1])
            elif t2 >= 2:
                wait("tensor", "V", tok_oadd[t2 - 2])
            for c in range(NC8):
                for js in JS2:
                    emit("tensor", lambda e, t=t2, c=c, js=js, rb=rb:
                         e.matmul(rb[:, js],
                                  vsb(t & 1)[:, c * P:(c + 1) * P],
                                  wa(SLOT_O)[:, c, js], start=(c == 0),
                                  stop=False))
            for ji, js in enumerate(JS2):
                fn = lambda e, js=js, rb=rb: e.matmul(
                    rb[:, js], ONES, brow(SLOT_O)[:, js], start=False,
                    stop=True)
                if ji == 1:
                    tok_attn[t2] = inc("tensor", "T", 1, fn)
                else:
                    emit("tensor", fn)

        def emit_oadd(t2):
            rb = R[2] if t2 % 2 == 0 else R[1]
            wait("vector", "T", tok_attn[t2])
            wait("vector", "S", tok_oinit[t2])
            tok_oadd[t2] = inc(
                "vector", "V", 1,
                lambda v, t=t2, rb=rb: v.tensor_tensor(
                    out_acc[:, t, :], rb, out_acc[:, t, :], ALU.add))

        tok_sg_t = [0] * NT

        def emit_snn_signs(t, cx):
            wait("scalar", "V", tok_y_ready[t])
            wait("scalar", "G", g_setup)
            if t >= 2:
                wait("scalar", "V", tok_a3[t - 2])
            if cx == 1 and t < 8:
                wait("scalar", "T", tok_pj_done[NSUP - 1])
            for j in range(5):
                fn = lambda sc, j=j, t=t, cx=cx: sc.sign(
                    sgv(cx, j), ya_v[:, t, :], bias=c_nth(j))
                if j == 4:
                    tok_sg_t[t] = inc("scalar", "S", 1, fn)
                else:
                    emit("scalar", fn)

        def emit_snn_adds1(t, cx):
            wait("vector", "S", tok_sg_t[t])
            emit("vector", lambda v, cx=cx: v.tensor_tensor(
                sgv(cx, 0), sgv(cx, 0), sgv(cx, 1), ALU.add))
            emit("vector", lambda v, cx=cx: v.tensor_tensor(
                sgv(cx, 2), sgv(cx, 2), sgv(cx, 3), ALU.add))
            emit("vector", lambda v, cx=cx: v.tensor_tensor(
                sgv(cx, 1), sgv(cx, 0), sgv(cx, 4), ALU.add))

        def emit_snn_adds2(t, cx):
            tok_a3[t] = inc("vector", "V", 1, lambda v, cx=cx:
                            v.tensor_tensor(sgv(cx, 3), sgv(cx, 2),
                                            sgv(cx, 1), ALU.add))

        def emit_snn_fdma(t, cx):
            wait("scalar", "V", tok_a3[t])
            if t >= 2:
                wait("scalar", "DO", tok_do2[t - 2])
            tok_fb[t] = inc("scalar", "S", 1, lambda sc, cx=cx:
                            sc.activation(fbv(cx), sgv(cx, 3), AF.Identity,
                                          bias=C_05, scale=0.1))
            wait("gpsimd", "V", tok_oadd[t])
            tok_do1 = inc("gpsimd", "DO", 16, lambda g, t=t: g.dma_start(
                out=out_hbm[t * P:(t + 1) * P, :], in_=out_acc[:, t, :]))
            wait("gpsimd", "DO", tok_do1)
            wait("gpsimd", "S", tok_fb[t])
            tok_do2[t] = inc("gpsimd", "DO", 16, lambda g, t=t, cx=cx:
                             g.dma_start(
                                 out=out_hbm[t * P:(t + 1) * P, :],
                                 in_=fbv(cx), accum_op=ALU.add))

        for pt in range(0, NT, 2):
            a, b = pt, pt + 1
            a2, b2 = pt - 2, pt - 1
            a4, b4 = pt - 4, pt - 3
            if pt >= 4:
                emit_snn_signs(a4, 0)
            emit_qproj(a)
            emit_rotB(a)
            if pt >= 4:
                emit_snn_adds1(a4, 0)
            if pt >= 2:
                emit_qtp(a2)
                emit_qst(a2)
            if pt >= 4:
                emit_snn_adds2(a4, 0)
            emit_sblkB(a)
            if pt >= 4:
                emit_snn_signs(b4, 1)
                emit_attn(a2 - 1)
                emit_oadd(a2 - 1)
                emit_snn_fdma(a4, 0)
            emit_qproj(b)
            emit_rotB(b)
            if pt >= 4:
                emit_snn_adds1(b4, 1)
            if pt >= 2:
                emit_qtp(b2)
                emit_qst(b2)
            if pt >= 4:
                emit_snn_adds2(b4, 1)
            emit_sblkB(b)
            if pt >= 2:
                emit_attn(b2 - 1)
                emit_oadd(b2 - 1)
            if pt >= 4:
                emit_snn_fdma(b4, 1)

        wait("vector", "S", tok_expq[NT - 1])
        tok_s16b[NT - 1] = inc("vector", "V", 1, lambda v: v.tensor_reduce(
            stv(NT - 1, ST_S16, NH),
            expb((NT - 1) & 1).rearrange("p (h d) -> p h d", d=HD),
            AX.X, ALU.add))
        wait("vector", "S", tok_cq[NT - 2])
        cq14 = stv(NT - 2, ST_CK, NH).rearrange(
            "p (h o) -> p h o", o=1).broadcast_to([P, NH, HD])
        wait("vector", "T", tok_qtp[NT - 4])
        tok_qs[NT - 2] = inc("vector", "V", 1, lambda v, cq14=cq14:
                             v.tensor_tensor(
                                 ksb(0).rearrange("p (h d) -> p h d", d=HD),
                                 expb(0).rearrange("p (h d) -> p h d", d=HD),
                                 cq14, ALU.mult))
        wait("scalar", "V", tok_s16b[NT - 1])
        emit("scalar", lambda s: s.activation(
            stv(NT - 1, ST_CK, NH), stv(NT - 1, ST_S16, NH), AF.Ln,
            scale=1.0 + EPS, bias=C_EPS2))
        emit("scalar", lambda s: s.drain())
        tok_cq[NT - 1] = inc("scalar", "S", 1, lambda s: s.activation(
            stv(NT - 1, ST_CK, NH), stv(NT - 1, ST_CK, NH), AF.Exp,
            scale=-1.0))
        wait("vector", "S", tok_cq[NT - 1])
        cq15 = stv(NT - 1, ST_CK, NH).rearrange(
            "p (h o) -> p h o", o=1).broadcast_to([P, NH, HD])
        tok_qs[NT - 1] = inc("vector", "V", 1, lambda v, cq15=cq15:
                             v.tensor_tensor(
                                 ksb(1).rearrange("p (h d) -> p h d", d=HD),
                                 expb(1).rearrange("p (h d) -> p h d", d=HD),
                                 cq15, ALU.mult))
        emit_snn_signs(NT - 4, 0)
        emit_snn_signs(NT - 3, 1)
        emit_snn_adds1(NT - 4, 0)
        emit_qtp(NT - 2)
        emit_qst(NT - 2)
        emit_snn_adds2(NT - 4, 0)
        emit_attn(NT - 3)
        emit_oadd(NT - 3)
        emit_snn_fdma(NT - 4, 0)
        emit_snn_adds1(NT - 3, 1)
        emit_qtp(NT - 1)
        emit_qst(NT - 1)
        emit_snn_adds2(NT - 3, 1)
        emit_attn(NT - 2)
        emit_oadd(NT - 2)
        emit_snn_fdma(NT - 3, 1)
        emit_snn_signs(NT - 2, 0)
        emit_snn_signs(NT - 1, 1)
        emit_snn_adds1(NT - 2, 0)
        emit_attn(NT - 1)
        emit_oadd(NT - 1)
        emit_snn_adds2(NT - 2, 0)
        emit_snn_fdma(NT - 2, 0)
        emit_snn_adds1(NT - 1, 1)
        emit("vector", lambda v: v.drain())
        emit_snn_adds2(NT - 1, 1)
        emit_snn_fdma(NT - 1, 1)

        final_do = cnt["DO"]
        emit("gpsimd", lambda g: g.wait_ge(sems["DO"], final_do))
        emit("sync", lambda e: e.wait_ge(sems["DO"], final_do))

        with nc.Block() as block:
            @block.sync
            def _(e):
                for fn in prog["sync"]:
                    fn(e)

            @block.tensor
            def _(e):
                for fn in prog["tensor"]:
                    fn(e)

            @block.vector
            def _(e):
                for fn in prog["vector"]:
                    fn(e)

            @block.scalar
            def _(e):
                for fn in prog["scalar"]:
                    fn(e)

            @block.gpsimd
            def _(e):
                for fn in prog["gpsimd"]:
                    fn(e)

    return nc


_NC_CACHE = None


def kernel(L0_attn_hidden_states, L0_attn_Wq, L0_attn_bq, L0_attn_Wk,
           L0_attn_bk, L0_attn_Wv, L0_attn_bv, L0_attn_Wo, L0_attn_bo,
           L0_ffn_W_fc, L0_ffn_W_proj):
    global _NC_CACHE
    x = np.ascontiguousarray(np.asarray(L0_attn_hidden_states,
                                        dtype=np.float32))
    base = {
        "wq": np.ascontiguousarray(np.asarray(L0_attn_Wq, dtype=np.float32)),
        "wk": np.ascontiguousarray(np.asarray(L0_attn_Wk, dtype=np.float32)),
        "wv": np.ascontiguousarray(np.asarray(L0_attn_Wv, dtype=np.float32)),
        "wo": np.ascontiguousarray(np.asarray(L0_attn_Wo, dtype=np.float32)),
        "bq": np.ascontiguousarray(
            np.asarray(L0_attn_bq, dtype=np.float32).reshape(1, D)),
        "bk": np.ascontiguousarray(
            np.asarray(L0_attn_bk, dtype=np.float32).reshape(1, D)),
        "bv": np.ascontiguousarray(
            np.asarray(L0_attn_bv, dtype=np.float32).reshape(1, D)),
        "bo": np.ascontiguousarray(
            np.asarray(L0_attn_bo, dtype=np.float32).reshape(1, D)),
        "wfc": np.ascontiguousarray(
            np.asarray(L0_ffn_W_fc, dtype=np.float32).T),
        "wpr": np.ascontiguousarray(
            np.asarray(L0_ffn_W_proj, dtype=np.float32).T),
    }
    if _NC_CACHE is None:
        _NC_CACHE = build_nc()
    nc = _NC_CACHE
    in_maps = [dict(base, x=np.ascontiguousarray(x[i])) for i in range(8)]
    res = run_bass_kernel_spmd(nc, in_maps, core_ids=list(range(8)))
    out = np.stack([np.asarray(res.results[i]["out"]) for i in range(8)],
                   axis=0)
    return out


# version 35
# speedup vs baseline: 1.0168x; 1.0168x over previous
import numpy as np
import concourse.bass as bass
import concourse.mybir as mybir
from concourse.bass_utils import run_bass_kernel_spmd
from contextlib import ExitStack

F32 = mybir.dt.float32
BF16 = mybir.dt.bfloat16
F16 = mybir.dt.float16
AF = mybir.ActivationFunctionType
ALU = mybir.AluOpType
AX = mybir.AxisListType

P = 128
NT = 16
D = 1024
NC8 = 8
NH = 16
HD = 64
NPAIR = 8
FF = 4096
NSUP = 4
EPS = 1e-6
LN_EPS = 1e-5

THETA = (0.24419428096993973, 0.4639691338428854, 0.6861859295255306,
         0.9059607823984763, 1.0839784132255623)

WQ_OFF = 0
WO_OFF = 16384
WK_OFF = 32768
WV_OFF = 49152
XBUF_OFF = 65536
SQ_OFF = 73728
XN_OFF = 81920
EXPA_OFF = 86016
KSA_OFF = 90112
VSA_OFF = 94208
YA_OFF = 65536
HT_OFF = 98304
EXPB_OFF = 98304
KSB_OFF = 102400
VSB_OFF = 106496
CTB_OFF = 110592
SG_OFFS = {0: (114688, 116736, 118784, 120832, 122880),
           1: (110592, 112640, 126976, 129024, 131072)}
FB_OFFS = {0: 124928, 1: 133120}
POOL_BYTES = 135168
POOL_F32 = POOL_BYTES // 4

SLOT_Q, SLOT_K, SLOT_V, SLOT_O = 0, 1, 2, 3
WA_OFFS = {SLOT_Q: WQ_OFF, SLOT_K: WK_OFF, SLOT_V: WV_OFF, SLOT_O: WO_OFF}

ST_SUM, ST_SQ, ST_MEAN, ST_EM2, ST_VEPS, ST_NMR, ST_T0, ST_T1 = range(8)
ST_NMAX, ST_S16, ST_CK = 8, 24, 40


def build_nc():
    nc = bass.Bass("TRN2", target_bir_lowering=False,
                   detect_race_conditions=False)

    x_hbm = nc.dram_tensor("x", [2048, D], F32, kind="ExternalInput")
    w_hbm = {
        SLOT_Q: nc.dram_tensor("wq", [D, D], F32, kind="ExternalInput"),
        SLOT_K: nc.dram_tensor("wk", [D, D], F32, kind="ExternalInput"),
        SLOT_V: nc.dram_tensor("wv", [D, D], F32, kind="ExternalInput"),
        SLOT_O: nc.dram_tensor("wo", [D, D], F32, kind="ExternalInput"),
    }
    b_hbm = {
        SLOT_Q: nc.dram_tensor("bq", [1, D], F32, kind="ExternalInput"),
        SLOT_K: nc.dram_tensor("bk", [1, D], F32, kind="ExternalInput"),
        SLOT_V: nc.dram_tensor("bv", [1, D], F32, kind="ExternalInput"),
        SLOT_O: nc.dram_tensor("bo", [1, D], F32, kind="ExternalInput"),
    }
    wfc_hbm = nc.dram_tensor("wfc", [D, FF], F32, kind="ExternalInput")
    wpr_hbm = nc.dram_tensor("wpr", [FF, D], F32, kind="ExternalInput")
    out_hbm = nc.dram_tensor("out", [2048, D], F32, kind="ExternalOutput")

    with ExitStack() as es:
        pool = es.enter_context(nc.sbuf_tensor("pool", [P, POOL_F32], F32))
        out_acc = es.enter_context(nc.sbuf_tensor("oacc", [P, NT, D], BF16))
        xnT = es.enter_context(nc.sbuf_tensor("xnT", [P, NC8, 2048], BF16))
        kvbd = es.enter_context(nc.sbuf_tensor("kvbd", [P, D], BF16))
        ident = es.enter_context(nc.sbuf_tensor("ident", [P, P], BF16))
        brows = es.enter_context(nc.sbuf_tensor("brows", [1, 4224], BF16))
        st = es.enter_context(nc.sbuf_tensor("st", [P, 2, 64], F32))
        cst = es.enter_context(nc.sbuf_tensor("cst", [P, 8], F32))
        dscr = None
        ppool = es.enter_context(nc.psum_tensor("ppool", [P, 4, 1024], F32))

        pool_b16 = pool[:].bitcast(BF16)
        pool_f16 = pool[:].bitcast(F16)
        pool_f32 = pool[:]

        def bfv(off, n):
            return pool_b16[:, off // 2: off // 2 + n]

        def f16v(off, n):
            return pool_f16[:, off // 2: off // 2 + n]

        def f32v(off, n):
            return pool_f32[:, off // 4: off // 4 + n]

        def wa(m):
            return bfv(WA_OFFS[m], NC8 * 1024).rearrange(
                "p (c n) -> p c n", n=1024)

        wf_v = bfv(WK_OFF, NC8 * 1024).rearrange("p (c n) -> p c n", n=1024)
        wp_v = bfv(WV_OFF, NC8 * 1024).rearrange("p (c n) -> p c n", n=1024)
        ht_v = bfv(HT_OFF, NC8 * 1024).rearrange("p (c n) -> p c n", n=1024)
        ht_f16 = f16v(HT_OFF, NC8 * 1024).rearrange("p (c n) -> p c n",
                                                    n=1024)
        ya_v = f16v(YA_OFF, NT * 1024).rearrange("p (c n) -> p c n", n=1024)

        def xbuf(sl):
            return f32v(XBUF_OFF + sl * 4096, 1024)

        def sqb(sl):
            return f32v(SQ_OFF + sl * 4096, 1024)

        def xnb(sl):
            return bfv(XN_OFF + sl * 2048, 1024)

        def expa(sl):
            return f16v(EXPA_OFF + sl * 2048, 1024)

        def ksa(sl):
            return bfv(KSA_OFF + sl * 2048, 1024)

        def vsa(sl):
            return bfv(VSA_OFF + sl * 2048, 1024)

        def expb(sl):
            return f16v(EXPB_OFF + sl * 2048, 1024)

        def ksb(sl):
            return bfv(KSB_OFF + sl * 2048, 1024)

        def vsb(sl):
            return bfv(VSB_OFF + sl * 2048, 1024)

        def ctb(sl):
            return bfv(CTB_OFF + sl * 2048, 1024)

        def sgv(ctx, j):
            return f16v(SG_OFFS[ctx][j], 1024)

        def fbv(ctx):
            return f16v(FB_OFFS[ctx], 1024)

        R = [ppool[:, i, :] for i in range(4)]
        PT = ppool[:].bitcast(BF16)[:, 3, 1024:2048]
        PT3 = PT.rearrange("p (c n) -> p c n", n=P)
        PTs = (ppool[:].bitcast(BF16)[:, 3, 0:1024], PT)

        def r3d(i):
            return ppool[:, i, :].rearrange("p (h d) -> p h d", d=HD)

        ONES = brows[0:1, 4096:4224]

        def brow(m):
            return brows[0:1, m * 1024:(m + 1) * 1024]

        C_05 = cst[:, 0:1]
        C_EPSA = cst[:, 6:7]
        C_EPS2 = cst[:, 7:8]

        def c_nth(j):
            return cst[:, 1 + j:2 + j]

        sem_names = (["T", "V", "S", "G", "DX", "DO", "DWB", "DWK", "DWV",
                      "DWQ", "DWO"]
                     + [f"DF{s}" for s in range(NSUP)]
                     + [f"DP{s}" for s in range(NSUP)])
        sems = {n: es.enter_context(nc.semaphore(n)) for n in sem_names}

        prog = {"sync": [], "tensor": [], "vector": [], "scalar": [],
                "gpsimd": []}
        cnt = {n: 0 for n in sem_names}
        lastw = {}

        def emit(eng, fn):
            prog[eng].append(fn)

        def inc(eng, sem, k, fn, k_total=None):
            cnt[sem] += k if k_total is None else k_total
            tok = cnt[sem]
            prog[eng].append(
                lambda e, fn=fn, sem=sem, k=k: fn(e).then_inc(sems[sem], k))
            return tok

        def wait(eng, sem, val):
            if val is None or val <= 0:
                return
            key = (eng, sem)
            if lastw.get(key, 0) >= val:
                return
            lastw[key] = val
            prog[eng].append(
                lambda e, sem=sem, val=val: e.wait_ge(sems[sem], val))

        def stv(t, col, n=1):
            return st[:, t & 1, col:col + n]

        emit("gpsimd", lambda g: g.memset(ident[:], 0.0))
        emit("gpsimd", lambda g: g.memset(kvbd[:], 0.0))
        emit("gpsimd", lambda g: g.memset(C_05, 0.5))
        emit("gpsimd", lambda g: g.memset(C_EPSA, EPS))
        emit("gpsimd", lambda g: g.memset(C_EPS2, EPS * EPS))
        for j in range(5):
            emit("gpsimd", lambda g, j=j: g.memset(c_nth(j), -THETA[j]))
        emit("gpsimd", lambda g: g.memset(st[:, 0, 56:58], 0.0))
        emit("gpsimd", lambda g: g.drain())
        emit("gpsimd", lambda g: g.affine_select(
            out=ident[:], in_=ident[:], compare_op=ALU.not_equal, fill=1.0,
            base=0, pattern=[[-1, P]], channel_multiplier=1))
        g_setup = inc("gpsimd", "G", 1, lambda g: g.memset(ONES, 1.0))

        DWB_TOK = {}
        for m, semn in ((SLOT_K, "DWK"), (SLOT_V, "DWV"), (SLOT_Q, "DWQ"),
                        (SLOT_O, "DWO")):
            for c in range(NC8):
                inc("gpsimd", semn, 16,
                    lambda g, m=m, c=c: g.dma_start(
                        out=wa(m)[:, c, :],
                        in_=w_hbm[m][c * P:(c + 1) * P, :]))
            inc("gpsimd", semn, 16,
                lambda g, m=m: g.dma_start(
                    out=f32v(133120 + m * 512, 128)[0:1, :],
                    in_=b_hbm[m][:, 0:128]))
            DWB_TOK[m] = inc("gpsimd", "DWB", 16,
                             lambda g, m=m: g.dma_start(out=brow(m),
                                                        in_=b_hbm[m][:, :]))
        tok_dwb = cnt["DWB"]

        tok_x = [0] * NT
        tok_sq = [0] * NT
        tok_oinit = [0] * NT
        tok_sqrd = [0] * NT
        tok_xn = [0] * NT
        tok_tp = [0] * NT
        tok_ptfree = [0] * NT
        tok_kproj = [0] * NT
        tok_r0free = [0] * NT
        tok_vproj = [0] * NT
        tok_expk = [0] * NT
        tok_s16 = [0] * NT
        tok_ck = [0] * NT
        tok_ks = [0] * NT
        tok_vs = [0] * NT
        tok_kv = [0] * NT

        def tg(lst, t):
            return lst[t] if t >= 0 else 0

        def emit_tp(t):
            wait("tensor", "S", tok_xn[t])
            wait("tensor", "G", g_setup)
            if t >= 1:
                wait("tensor", "S", tok_ptfree[t - 1])
            for c in range(NC8):
                fn = lambda e, t=t, c=c: e.transpose(
                    PT[:, c * P:(c + 1) * P],
                    xnb(t & 1)[:, c * P:(c + 1) * P], ident[:])
                if c < NC8 - 1:
                    emit("tensor", fn)
                else:
                    tok_tp[t] = inc("tensor", "T", 1, fn)

        JS2 = (slice(0, 512), slice(512, 1024))

        WSEM = {SLOT_K: "DWK", SLOT_V: "DWV", SLOT_Q: "DWQ",
                SLOT_O: "DWO"}
        WOFF = {SLOT_K: 0, SLOT_V: 0, SLOT_Q: 0, SLOT_O: 0}

        def emit_projmm(t, slot, rdst, tok_list):
            for c in range(NC8):
                wait("tensor", WSEM[slot], 16 * (c + 2))
                for js in JS2:
                    emit("tensor", lambda e, t=t, c=c, js=js, slot=slot,
                         rdst=rdst: e.matmul(
                             R[rdst][:, js], xnT[:, c, t * P:(t + 1) * P],
                             wa(slot)[:, c, js], start=(c == 0), stop=False))
            wait("tensor", "DWB", DWB_TOK[slot])
            for ji, js in enumerate(JS2):
                fn = lambda e, js=js, slot=slot, rdst=rdst: e.matmul(
                    R[rdst][:, js], ONES, brow(slot)[:, js], start=False,
                    stop=True)
                if ji == 1:
                    tok_list[t] = inc("tensor", "T", 1, fn)
                else:
                    emit("tensor", fn)

        def emit_kv(t):
            wait("tensor", "V", tok_ks[t])
            wait("tensor", "V", tok_vs[t])
            for p_ in range(NPAIR):
                ps = slice(p_ * P, (p_ + 1) * P)
                fn = lambda e, t=t, ps=ps, p_=p_: e.matmul(
                    R[2][:, ps], ksa(t & 1)[:, ps], vsa(t & 1)[:, ps],
                    start=(t == 0 and p_ % 4 == 0), stop=(t == NT - 1),
                    skip_group_check=True)
                if p_ == NPAIR - 1:
                    tok_kv[t] = inc("tensor", "T", 1, fn)
                else:
                    emit("tensor", fn)

        def emit_rotA(t):
            if t >= 2:
                wait("vector", "S", tok_ck[t - 2])
                if t >= 4:
                    wait("vector", "T", tok_kv[t - 4])
                ck2 = stv(t - 2, ST_CK, NH).rearrange(
                    "p (h o) -> p h o", o=1).broadcast_to([P, NH, HD])
                tok_ks[t - 2] = inc(
                    "vector", "V", 1, lambda v, t=t, ck2=ck2: v.tensor_tensor(
                        ksa(t & 1).rearrange("p (h d) -> p h d", d=HD),
                        expa(t & 1).rearrange("p (h d) -> p h d", d=HD),
                        ck2, ALU.mult))
            wait("vector", "T", tok_kproj[t])
            emit("vector", lambda v, t=t: v.tensor_reduce(
                stv(t, ST_NMAX, NH), r3d(0), AX.X, ALU.max, negate=True))
            nmb = stv(t, ST_NMAX, NH).rearrange(
                "p (h o) -> p h o", o=1).broadcast_to([P, NH, HD])
            tok_r0free[t] = inc(
                "vector", "V", 1, lambda v, t=t, nmb=nmb: v.tensor_tensor(
                    expa(t & 1).rearrange("p (h d) -> p h d", d=HD), r3d(0),
                    nmb, ALU.add))
            if t >= 1:
                wait("vector", "S", tok_expk[t - 1])
                tok_s16[t - 1] = inc("vector", "V", 1, lambda v, t=t:
                                     v.tensor_reduce(
                                         stv(t - 1, ST_S16, NH),
                                         expa((t - 1) & 1).rearrange(
                                             "p (h d) -> p h d", d=HD),
                                         AX.X, ALU.add))

        def emit_vsA(t):
            wait("vector", "T", tok_vproj[t])
            if t >= 2:
                assert tok_kv[t - 2] > 0
                wait("vector", "T", tok_kv[t - 2])
            tok_vs[t] = inc("vector", "V", 1,
                            lambda v, t=t: v.tensor_copy(vsa(t & 1), R[1]))

        def emit_sblkA(t):
            wait("scalar", "V", tok_r0free[t])
            tok_expk[t] = inc("scalar", "S", 1, lambda s, t=t: s.activation(
                expa(t & 1), expa(t & 1), AF.Exp))
            if t >= 1:
                wait("scalar", "V", tok_s16[t - 1])
                emit("scalar", lambda s, t=t: s.activation(
                    stv(t - 1, ST_CK, NH), stv(t - 1, ST_S16, NH), AF.Ln,
                    bias=C_EPSA))
                emit("scalar", lambda s: s.copy(
                    st[:, 0, 56:57], st[:, 0, 57:58]))
                tok_ck[t - 1] = inc("scalar", "S", 1, lambda s, t=t:
                                    s.activation(
                                        stv(t - 1, ST_CK, NH),
                                        stv(t - 1, ST_CK, NH), AF.Exp,
                                        scale=-1.0))

        def emit_headA(pt):
            if pt >= NT:
                return
            a, b = pt, pt + 1
            for t in (a, b):
                if t >= 2:
                    wait("sync", "S", tok_xn[t - 2])
                    wait("sync", "V", tok_sqrd[t - 2])
                    wait("sync", "S", tok_oinit[t - 2])
                if t >= 1:
                    wait("sync", "DX", tok_x[t - 1])
                tok_x[t] = inc("sync", "DX", 16,
                               lambda e, t=t: e.dma_start(
                                   out=xbuf(t & 1),
                                   in_=x_hbm[t * P:(t + 1) * P, :]))
            for t in (a, b):
                wait("scalar", "DX", tok_x[t])
                if t >= 2:
                    wait("scalar", "V", tok_sqrd[t - 2])
            tok_sq[a] = inc("scalar", "S", 1,
                            lambda s, a=a: s.square(sqb(a & 1), xbuf(a & 1)))
            tok_sq[b] = inc("scalar", "S", 1,
                            lambda s, b=b: s.square(sqb(b & 1), xbuf(b & 1)))
            for t in (a, b):
                tok_oinit[t] = inc(
                    "scalar", "S", 1,
                    lambda s, t=t: s.copy(out_acc[:, t, :], xbuf(t & 1)))
            for t in (a, b):
                wait("vector", "DX", tok_x[t])
                emit("vector", lambda v, t=t: v.tensor_reduce(
                    stv(t, ST_SUM), xbuf(t & 1), AX.X, ALU.add))
            for t in (a, b):
                wait("vector", "S", tok_sq[t])
                tok_sqrd[t] = inc("vector", "V", 1, lambda v, t=t:
                                  v.tensor_reduce(stv(t, ST_SQ), sqb(t & 1),
                                                  AX.X, ALU.add))
            for t in (a, b):
                emit("vector", lambda v, t=t: v.tensor_scalar(
                    stv(t, ST_MEAN), stv(t, ST_SUM), -1.0 / D, None,
                    ALU.mult))
            for t in (a, b):
                emit("vector", lambda v, t=t: v.tensor_scalar(
                    stv(t, ST_EM2), stv(t, ST_SQ), 1.0 / D, None, ALU.mult))
            for t in (a, b):
                emit("vector", lambda v, t=t: v.tensor_scalar(
                    stv(t, ST_T0), stv(t, ST_MEAN), stv(t, ST_MEAN), -1.0,
                    ALU.mult, ALU.mult))
            tok_veps = {}
            for t in (a, b):
                tok_veps[t] = inc("vector", "V", 1, lambda v, t=t:
                                  v.tensor_scalar(
                                      stv(t, ST_VEPS), stv(t, ST_EM2),
                                      stv(t, ST_T0), LN_EPS, ALU.add,
                                      ALU.add))
            for t in (a, b):
                wait("scalar", "V", tok_veps[t])
                emit("scalar", lambda s, t=t: s.activation(
                    stv(t, ST_T1), stv(t, ST_VEPS), AF.Ln))
            for t in (a, b):
                emit("scalar", lambda s, t=t: s.activation(
                    stv(t, ST_VEPS), stv(t, ST_T1), AF.Exp, scale=-0.5))
            for t in (a, b):
                emit("scalar", lambda s, t=t: s.activation(
                    stv(t, ST_NMR), stv(t, ST_MEAN), AF.Identity,
                    scale=stv(t, ST_VEPS)))
            for t in (a, b):
                if t >= 2:
                    wait("scalar", "T", tok_tp[t - 2])
                tok_xn[t] = inc("scalar", "S", 1, lambda s, t=t:
                                s.activation(xnb(t & 1), xbuf(t & 1),
                                             AF.Identity,
                                             bias=stv(t, ST_NMR),
                                             scale=stv(t, ST_VEPS)))

        emit_headA(0)
        for pt in range(0, NT, 2):
            a, b = pt, pt + 1

            emit_tp(a)
            wait("scalar", "T", tok_tp[a])
            tok_ptfree[a] = inc(
                "scalar", "S", 1,
                lambda s, a=a: s.copy(xnT[:, :, a * P:(a + 1) * P], PT3))
            if pt == 0:
                emit_tp(b)
                wait("scalar", "T", tok_tp[b])
                tok_ptfree[b] = inc(
                    "scalar", "S", 1,
                    lambda s, b=b: s.copy(xnT[:, :, b * P:(b + 1) * P], PT3))
            wait("tensor", "S", tok_ptfree[a])
            wait("tensor", "V", tg(tok_r0free, a - 1))
            emit_projmm(a, SLOT_K, 0, tok_kproj)
            if pt != 0:
                emit_tp(b)
            wait("tensor", "V", tg(tok_vs, a - 1))
            emit_projmm(a, SLOT_V, 1, tok_vproj)
            if pt != 0:
                wait("scalar", "T", tok_tp[b])
                tok_ptfree[b] = inc(
                    "scalar", "S", 1,
                    lambda s, b=b: s.copy(xnT[:, :, b * P:(b + 1) * P], PT3))
            emit_rotA(a)
            if pt >= 2:
                emit_kv(a - 2)
            emit_vsA(a)
            emit_sblkA(a)
            emit_headA(pt + 2)
            wait("tensor", "S", tok_ptfree[b])
            wait("tensor", "V", tok_r0free[a])
            emit_projmm(b, SLOT_K, 0, tok_kproj)
            wait("tensor", "V", tok_vs[a])
            emit_projmm(b, SLOT_V, 1, tok_vproj)
            emit_rotA(b)
            if pt >= 2:
                emit_kv(b - 2)
            emit_vsA(b)
            emit_sblkA(b)

        wait("vector", "S", tok_expk[NT - 1])
        tok_s16[NT - 1] = inc("vector", "V", 1, lambda v: v.tensor_reduce(
            stv(NT - 1, ST_S16, NH),
            expa((NT - 1) & 1).rearrange("p (h d) -> p h d", d=HD),
            AX.X, ALU.add))
        wait("vector", "S", tok_ck[NT - 2])
        ck14 = stv(NT - 2, ST_CK, NH).rearrange(
            "p (h o) -> p h o", o=1).broadcast_to([P, NH, HD])
        tok_ks[NT - 2] = inc("vector", "V", 1, lambda v, ck14=ck14:
                             v.tensor_tensor(
                                 ksa(0).rearrange("p (h d) -> p h d", d=HD),
                                 expa(0).rearrange("p (h d) -> p h d", d=HD),
                                 ck14, ALU.mult))
        wait("scalar", "V", tok_s16[NT - 1])
        emit("scalar", lambda s: s.activation(
            stv(NT - 1, ST_CK, NH), stv(NT - 1, ST_S16, NH), AF.Ln,
            bias=C_EPSA))
        emit("scalar", lambda s: s.drain())
        tok_ck[NT - 1] = inc("scalar", "S", 1, lambda s: s.activation(
            stv(NT - 1, ST_CK, NH), stv(NT - 1, ST_CK, NH), AF.Exp,
            scale=-1.0))
        wait("vector", "S", tok_ck[NT - 1])
        ck15 = stv(NT - 1, ST_CK, NH).rearrange(
            "p (h o) -> p h o", o=1).broadcast_to([P, NH, HD])
        tok_ks[NT - 1] = inc("vector", "V", 1, lambda v, ck15=ck15:
                             v.tensor_tensor(
                                 ksa(1).rearrange("p (h d) -> p h d", d=HD),
                                 expa(1).rearrange("p (h d) -> p h d", d=HD),
                                 ck15, ALU.mult))
        emit_kv(NT - 2)
        emit_kv(NT - 1)

        wait("scalar", "T", tok_kv[NT - 1])
        tok_kvbd = None
        for p_ in range(NPAIR):
            c0 = p_ * P
            emit("scalar", lambda s, c0=c0: s.copy(
                kvbd[0:HD, c0:c0 + HD], R[2][0:HD, c0:c0 + HD]))
            fn = lambda s, c0=c0: s.copy(
                kvbd[HD:P, c0 + HD:c0 + P], R[2][HD:P, c0 + HD:c0 + P])
            if p_ == NPAIR - 1:
                tok_kvbd = inc("scalar", "S", 1, fn)
            else:
                emit("scalar", fn)

        wait("tensor", "S", tok_kvbd)
        wait("tensor", "S", tok_ptfree[NT - 1])
        wait("tensor", "DWO", 16 * NC8)
        for p_ in range(NPAIR):
            fn = lambda e, p_=p_: e.transpose(
                PT[:, p_ * P:(p_ + 1) * P], kvbd[:, p_ * P:(p_ + 1) * P],
                ident[:])
            if p_ == NPAIR - 1:
                tok_mtp = inc("tensor", "T", 1, fn)
            else:
                emit("tensor", fn)
        wait("scalar", "T", tok_mtp)
        tok_kvT = inc("scalar", "S", 1, lambda s: s.copy(ksa(0), PT))
        wait("tensor", "S", tok_kvT)
        wait("tensor", "V", tok_r0free[NT - 1])
        wait("tensor", "V", tok_vs[NT - 1])
        tok_m_ev = {0: None, 1: None}
        for p_ in range(NPAIR):
            rm = p_ & 1
            if tok_m_ev[rm] is not None:
                wait("tensor", "S", tok_m_ev[rm])
            last = None
            for jh in range(2):
                js = slice(jh * 512, (jh + 1) * 512)
                fn = lambda e, p_=p_, js=js, rm=rm: e.matmul(
                    R[rm][:, js], ksa(0)[:, p_ * P:(p_ + 1) * P],
                    wa(SLOT_O)[:, p_, js], start=True, stop=True,
                    skip_group_check=True)
                if jh == 1:
                    last = inc("tensor", "T", 1, fn)
                else:
                    emit("tensor", fn)
            wait("scalar", "T", last)
            tok_m_ev[rm] = inc("scalar", "S", 1,
                               lambda s, p_=p_, rm=rm: s.copy(
                                   wa(SLOT_O)[:, p_, :], R[rm]))

        tok_df = [0] * NSUP
        tok_dp = [0] * NSUP
        tok_fc_done = [0] * NSUP
        tok_pj_done = [0] * NSUP

        def emit_wf_dma(s):
            wait("gpsimd", "T",
                 tok_kproj[NT - 1] if s == 0 else tok_fc_done[s - 1])
            src = wfc_hbm[:, s * 1024:(s + 1) * 1024].rearrange(
                "(c p) n -> p c n", p=P)
            inc("gpsimd", f"DF{s}", 16,
                lambda g, src=src: g.dma_start(out=wf_v, in_=src))
            tok_df[s] = cnt[f"DF{s}"]

        def emit_wp_dma(s):
            wait("gpsimd", "T",
                 tok_vproj[NT - 1] if s == 0 else tok_pj_done[s - 1])
            src = wpr_hbm[s * 1024:(s + 1) * 1024, :].rearrange(
                "(c p) n -> p c n", p=P)
            inc("gpsimd", f"DP{s}", 16,
                lambda g, src=src: g.dma_start(out=wp_v, in_=src))
            tok_dp[s] = cnt[f"DP{s}"]

        emit_wf_dma(0)
        emit_wp_dma(0)

        fc_ctr = 0
        tok_pa_free = {0: tok_m_ev[0], 1: tok_m_ev[1]}
        tok_pa_kind = {0: "S", 1: "S"}
        tok_py_free = {2: tok_kvbd, 3: tok_kvT}
        tok_py_kind = {2: "S", 3: "S"}
        tok_y_ready = [0] * NT
        tok_pj_done_prev = 0
        tok_ht_last = None

        for s in range(NSUP):
            tok_relu_f = [0] * NC8
            for H in range(2):
                prev_sq = None
                for fcc in range(NC8):
                    pa = fc_ctr & 1
                    fc_ctr += 1
                    wait("tensor", f"DF{s}", tok_df[s])
                    wait("tensor", tok_pa_kind[pa], tok_pa_free[pa])
                    last = None
                    for c in range(NC8):
                        for jh in range(2):
                            js = slice(jh * 512, (jh + 1) * 512)
                            fn = (lambda e, pa=pa, c=c, fcc=fcc, H=H, js=js,
                                  jh=jh:
                                  e.matmul(R[pa][:, js],
                                           wf_v[:, c, fcc * P:(fcc + 1) * P],
                                           xnT[:, c, H * 1024 + jh * 512:
                                               H * 1024 + (jh + 1) * 512],
                                           start=(c == 0),
                                           stop=(c == NC8 - 1)))
                            if jh == 1 and c == NC8 - 1:
                                last = inc("tensor", "T", 1, fn)
                            else:
                                emit("tensor", fn)
                    if H == 1 and fcc == NC8 - 1:
                        tok_fc_done[s] = last

                    wait("scalar", "T", last)
                    wait("scalar", "T", tok_pj_done_prev)
                    tok_pa_free[pa] = inc(
                        "scalar", "S", 1,
                        lambda sc, pa=pa, fcc=fcc: sc.activation(
                            ht_f16[:, fcc, :], R[pa], AF.Relu))
                    tok_pa_kind[pa] = "S"
                    tok_relu_f[fcc] = tok_pa_free[pa]
                    if prev_sq is not None:
                        ppa, pfcc = prev_sq
                        wait("vector", "S", tok_relu_f[pfcc])
                        emit("vector", lambda v, pfcc=pfcc: v.tensor_tensor(
                            ht_v[:, pfcc, :], ht_f16[:, pfcc, :],
                            ht_f16[:, pfcc, :], ALU.mult))
                    prev_sq = (pa, fcc)
                ppa, pfcc = prev_sq
                wait("vector", "S", tok_relu_f[pfcc])
                tok_ht_ready = inc(
                    "vector", "V", 1,
                    lambda v, pfcc=pfcc: v.tensor_tensor(
                        ht_v[:, pfcc, :], ht_f16[:, pfcc, :],
                        ht_f16[:, pfcc, :], ALU.mult))
                tok_ht_last = tok_ht_ready

                for ts_ in range(8):
                    tglob = H * 8 + ts_
                    py = 2 + (ts_ & 1)
                    wait("tensor", f"DP{s}", tok_dp[s])
                    wait("tensor", "V", tok_ht_ready)
                    wait("tensor", tok_py_kind[py], tok_py_free[py])
                    last = None
                    for fcc in range(NC8):
                        for jh in range(2):
                            js = slice(jh * 512, (jh + 1) * 512)
                            fn = (lambda e, py=py, fcc=fcc, js=js, ts_=ts_:
                                  e.matmul(
                                      R[py][:, js],
                                      ht_v[:, fcc, ts_ * P:(ts_ + 1) * P],
                                      wp_v[:, fcc, js],
                                      start=(fcc == 0), stop=(fcc == NC8 - 1)))
                            if fcc == NC8 - 1 and jh == 1:
                                last = inc("tensor", "T", 1, fn)
                            else:
                                emit("tensor", fn)
                    if H == 1 and ts_ == 7:
                        tok_pj_done[s] = last

                    if s == 0:
                        wait("scalar", "T", last)
                        wait("scalar", "V", tok_ks[NT - 1])
                        tok_py_free[py] = inc(
                            "scalar", "S", 1,
                            lambda sc, py=py, tglob=tglob: sc.copy(
                                ya_v[:, tglob, :], R[py]))
                        tok_py_kind[py] = "S"
                    else:
                        wait("vector", "T", last)
                        tok_py_free[py] = inc(
                            "vector", "V", 1,
                            lambda v, py=py, tglob=tglob: v.tensor_tensor(
                                ya_v[:, tglob, :], R[py], ya_v[:, tglob, :],
                                ALU.add))
                        tok_py_kind[py] = "V"
                        if s == NSUP - 1:
                            tok_y_ready[tglob] = tok_py_free[py]
                tok_pj_done_prev = last
            if s + 1 < NSUP:
                emit_wf_dma(s + 1)
                emit_wp_dma(s + 1)

        tok_qproj = [0] * NT
        tok_r0b = [0] * NT
        tok_expq = [0] * NT
        tok_s16b = [0] * NT
        tok_cq = [0] * NT
        tok_qs = [0] * NT
        tok_qtp = [0] * NT
        tok_qst = [0] * NT
        tok_ctx = [0] * NT
        tok_ctxe = [0] * NT
        tok_attn = [0] * NT
        tok_oadd = [0] * NT
        tok_a3 = [0] * NT
        tok_asum = [0] * NT
        tok_fb = [0] * NT
        tok_do2 = [0] * NT

        def emit_qproj(t):
            if t == 0:
                wait("tensor", tok_pa_kind[0], tok_pa_free[0])
                wait("tensor", tok_pa_kind[1], tok_pa_free[1])
            else:
                wait("tensor", "V", tok_r0b[t - 1])
            emit_projmm(t, SLOT_Q, 0, tok_qproj)

        def emit_rotB(t):
            if t >= 2:
                wait("vector", "S", tok_cq[t - 2])
                if t >= 4:
                    wait("vector", "T", tok_qtp[t - 4])
                cq2 = stv(t - 2, ST_CK, NH).rearrange(
                    "p (h o) -> p h o", o=1).broadcast_to([P, NH, HD])
                tok_qs[t - 2] = inc(
                    "vector", "V", 1, lambda v, t=t, cq2=cq2: v.tensor_tensor(
                        ksb(t & 1).rearrange("p (h d) -> p h d", d=HD),
                        expb(t & 1).rearrange("p (h d) -> p h d", d=HD),
                        cq2, ALU.mult))
            wait("vector", "T", tok_qproj[t])
            emit("vector", lambda v, t=t: v.tensor_reduce(
                stv(t, ST_NMAX, NH), r3d(0), AX.X, ALU.max, negate=True))
            nmb = stv(t, ST_NMAX, NH).rearrange(
                "p (h o) -> p h o", o=1).broadcast_to([P, NH, HD])
            tok_r0b[t] = inc(
                "vector", "V", 1, lambda v, t=t, nmb=nmb: v.tensor_tensor(
                    expb(t & 1).rearrange("p (h d) -> p h d", d=HD), r3d(0),
                    nmb, ALU.add))
            if t >= 1:
                wait("vector", "S", tok_expq[t - 1])
                tok_s16b[t - 1] = inc("vector", "V", 1, lambda v, t=t:
                                      v.tensor_reduce(
                                          stv(t - 1, ST_S16, NH),
                                          expb((t - 1) & 1).rearrange(
                                              "p (h d) -> p h d", d=HD),
                                          AX.X, ALU.add))

        def emit_sblkB(t):
            wait("scalar", "V", tok_r0b[t])
            tok_expq[t] = inc("scalar", "S", 1, lambda s, t=t: s.activation(
                expb(t & 1), expb(t & 1), AF.Exp))
            if t >= 1:
                wait("scalar", "V", tok_s16b[t - 1])
                emit("scalar", lambda s, t=t: s.activation(
                    stv(t - 1, ST_CK, NH), stv(t - 1, ST_S16, NH), AF.Ln,
                    scale=1.0 + EPS, bias=C_EPS2))
                emit("scalar", lambda s: s.copy(
                    st[:, 0, 56:57], st[:, 0, 57:58]))
                tok_cq[t - 1] = inc("scalar", "S", 1, lambda s, t=t:
                                    s.activation(
                                        stv(t - 1, ST_CK, NH),
                                        stv(t - 1, ST_CK, NH), AF.Exp,
                                        scale=-1.0))

        def emit_qtp(t2):
            wait("tensor", "V", tok_qs[t2])
            if t2 < 2:
                wait("tensor", "V", tok_y_ready[NT - 1])
            else:
                wait("tensor", "V", tok_qst[t2 - 2])
            for c in range(NC8):
                fn = lambda e, t=t2, c=c: e.transpose(
                    PTs[t & 1][:, c * P:(c + 1) * P],
                    ksb(t & 1)[:, c * P:(c + 1) * P], ident[:])
                if c < NC8 - 1:
                    emit("tensor", fn)
                else:
                    tok_qtp[t2] = inc("tensor", "T", 1, fn)

        def emit_qst(t2):
            wait("vector", "T", tok_qtp[t2])
            if t2 >= 2:
                wait("vector", "T", tok_attn[t2 - 2])
            tok_qst[t2] = inc("vector", "V", 1,
                              lambda v, t=t2: v.tensor_copy(vsb(t & 1),
                                                            PTs[t & 1]))

        def emit_attn(t2):
            rb = R[2] if t2 % 2 == 0 else R[1]
            wait("tensor", "V", tok_qst[t2])
            if t2 == 0:
                wait("tensor", "V", tok_y_ready[NT - 1])
            elif t2 >= 2:
                wait("tensor", "V", tok_oadd[t2 - 2])
            for c in range(NC8):
                for js in JS2:
                    emit("tensor", lambda e, t=t2, c=c, js=js, rb=rb:
                         e.matmul(rb[:, js],
                                  vsb(t & 1)[:, c * P:(c + 1) * P],
                                  wa(SLOT_O)[:, c, js], start=(c == 0),
                                  stop=False))
            for ji, js in enumerate(JS2):
                fn = lambda e, js=js, rb=rb: e.matmul(
                    rb[:, js], ONES, brow(SLOT_O)[:, js], start=False,
                    stop=True)
                if ji == 1:
                    tok_attn[t2] = inc("tensor", "T", 1, fn)
                else:
                    emit("tensor", fn)

        def emit_oadd(t2):
            rb = R[2] if t2 % 2 == 0 else R[1]
            wait("vector", "T", tok_attn[t2])
            wait("vector", "S", tok_oinit[t2])
            tok_oadd[t2] = inc(
                "vector", "V", 1,
                lambda v, t=t2, rb=rb: v.tensor_tensor(
                    out_acc[:, t, :], rb, out_acc[:, t, :], ALU.add))

        tok_sg_t = [0] * NT

        def emit_snn_signs(t, cx):
            wait("scalar", "V", tok_y_ready[t])
            wait("scalar", "G", g_setup)
            if t >= 2:
                wait("scalar", "V", tok_a3[t - 2])
            if cx == 1 and t < 8:
                wait("scalar", "T", tok_pj_done[NSUP - 1])
            for j in range(5):
                fn = lambda sc, j=j, t=t, cx=cx: sc.sign(
                    sgv(cx, j), ya_v[:, t, :], bias=c_nth(j))
                if j == 4:
                    tok_sg_t[t] = inc("scalar", "S", 1, fn)
                else:
                    emit("scalar", fn)

        def emit_snn_adds1(t, cx):
            wait("vector", "S", tok_sg_t[t])
            emit("vector", lambda v, cx=cx: v.tensor_tensor(
                sgv(cx, 0), sgv(cx, 0), sgv(cx, 1), ALU.add))
            emit("vector", lambda v, cx=cx: v.tensor_tensor(
                sgv(cx, 2), sgv(cx, 2), sgv(cx, 3), ALU.add))
            emit("vector", lambda v, cx=cx: v.tensor_tensor(
                sgv(cx, 1), sgv(cx, 0), sgv(cx, 4), ALU.add))

        def emit_snn_adds2(t, cx):
            tok_a3[t] = inc("vector", "V", 1, lambda v, cx=cx:
                            v.tensor_tensor(sgv(cx, 3), sgv(cx, 2),
                                            sgv(cx, 1), ALU.add))

        def emit_snn_fdma(t, cx):
            wait("scalar", "V", tok_a3[t])
            if t >= 2:
                wait("scalar", "DO", tok_do2[t - 2])
            tok_fb[t] = inc("scalar", "S", 1, lambda sc, cx=cx:
                            sc.activation(fbv(cx), sgv(cx, 3), AF.Identity,
                                          bias=C_05, scale=0.1))
            wait("gpsimd", "V", tok_oadd[t])
            tok_do1 = inc("gpsimd", "DO", 16, lambda g, t=t: g.dma_start(
                out=out_hbm[t * P:(t + 1) * P, :], in_=out_acc[:, t, :]))
            wait("gpsimd", "DO", tok_do1)
            wait("gpsimd", "S", tok_fb[t])
            tok_do2[t] = inc("gpsimd", "DO", 16, lambda g, t=t, cx=cx:
                             g.dma_start(
                                 out=out_hbm[t * P:(t + 1) * P, :],
                                 in_=fbv(cx), accum_op=ALU.add))

        for pt in range(0, NT, 2):
            a, b = pt, pt + 1
            a2, b2 = pt - 2, pt - 1
            a4, b4 = pt - 4, pt - 3
            if pt >= 4:
                emit_snn_signs(a4, 0)
            emit_qproj(a)
            emit_rotB(a)
            if pt >= 4:
                emit_snn_adds1(a4, 0)
            if pt >= 2:
                emit_qtp(a2)
                emit_qst(a2)
            if pt >= 4:
                emit_snn_adds2(a4, 0)
            emit_sblkB(a)
            if pt >= 4:
                emit_snn_signs(b4, 1)
                emit_attn(a2 - 1)
                emit_oadd(a2 - 1)
                emit_snn_fdma(a4, 0)
            emit_qproj(b)
            emit_rotB(b)
            if pt >= 4:
                emit_snn_adds1(b4, 1)
            if pt >= 2:
                emit_qtp(b2)
                emit_qst(b2)
            if pt >= 4:
                emit_snn_adds2(b4, 1)
            emit_sblkB(b)
            if pt >= 2:
                emit_attn(b2 - 1)
                emit_oadd(b2 - 1)
            if pt >= 4:
                emit_snn_fdma(b4, 1)

        wait("vector", "S", tok_expq[NT - 1])
        tok_s16b[NT - 1] = inc("vector", "V", 1, lambda v: v.tensor_reduce(
            stv(NT - 1, ST_S16, NH),
            expb((NT - 1) & 1).rearrange("p (h d) -> p h d", d=HD),
            AX.X, ALU.add))
        wait("vector", "S", tok_cq[NT - 2])
        cq14 = stv(NT - 2, ST_CK, NH).rearrange(
            "p (h o) -> p h o", o=1).broadcast_to([P, NH, HD])
        wait("vector", "T", tok_qtp[NT - 4])
        tok_qs[NT - 2] = inc("vector", "V", 1, lambda v, cq14=cq14:
                             v.tensor_tensor(
                                 ksb(0).rearrange("p (h d) -> p h d", d=HD),
                                 expb(0).rearrange("p (h d) -> p h d", d=HD),
                                 cq14, ALU.mult))
        wait("scalar", "V", tok_s16b[NT - 1])
        emit("scalar", lambda s: s.activation(
            stv(NT - 1, ST_CK, NH), stv(NT - 1, ST_S16, NH), AF.Ln,
            scale=1.0 + EPS, bias=C_EPS2))
        emit("scalar", lambda s: s.drain())
        tok_cq[NT - 1] = inc("scalar", "S", 1, lambda s: s.activation(
            stv(NT - 1, ST_CK, NH), stv(NT - 1, ST_CK, NH), AF.Exp,
            scale=-1.0))
        wait("vector", "S", tok_cq[NT - 1])
        cq15 = stv(NT - 1, ST_CK, NH).rearrange(
            "p (h o) -> p h o", o=1).broadcast_to([P, NH, HD])
        tok_qs[NT - 1] = inc("vector", "V", 1, lambda v, cq15=cq15:
                             v.tensor_tensor(
                                 ksb(1).rearrange("p (h d) -> p h d", d=HD),
                                 expb(1).rearrange("p (h d) -> p h d", d=HD),
                                 cq15, ALU.mult))
        emit_snn_signs(NT - 4, 0)
        emit_snn_signs(NT - 3, 1)
        emit_snn_adds1(NT - 4, 0)
        emit_qtp(NT - 2)
        emit_qst(NT - 2)
        emit_snn_adds2(NT - 4, 0)
        emit_attn(NT - 3)
        emit_oadd(NT - 3)
        emit_snn_fdma(NT - 4, 0)
        emit_snn_adds1(NT - 3, 1)
        emit_qtp(NT - 1)
        emit_qst(NT - 1)
        emit_snn_adds2(NT - 3, 1)
        emit_attn(NT - 2)
        emit_oadd(NT - 2)
        emit_snn_fdma(NT - 3, 1)
        emit_snn_signs(NT - 2, 0)
        emit_snn_signs(NT - 1, 1)
        emit_snn_adds1(NT - 2, 0)
        emit_attn(NT - 1)
        emit_oadd(NT - 1)
        emit_snn_adds2(NT - 2, 0)
        emit_snn_fdma(NT - 2, 0)
        emit_snn_adds1(NT - 1, 1)
        emit("vector", lambda v: v.drain())
        emit_snn_adds2(NT - 1, 1)
        emit_snn_fdma(NT - 1, 1)

        final_do = cnt["DO"]
        emit("gpsimd", lambda g: g.wait_ge(sems["DO"], final_do))
        emit("sync", lambda e: e.wait_ge(sems["DO"], final_do))

        with nc.Block() as block:
            @block.sync
            def _(e):
                for fn in prog["sync"]:
                    fn(e)

            @block.tensor
            def _(e):
                for fn in prog["tensor"]:
                    fn(e)

            @block.vector
            def _(e):
                for fn in prog["vector"]:
                    fn(e)

            @block.scalar
            def _(e):
                for fn in prog["scalar"]:
                    fn(e)

            @block.gpsimd
            def _(e):
                for fn in prog["gpsimd"]:
                    fn(e)

    return nc


_NC_CACHE = None


def kernel(L0_attn_hidden_states, L0_attn_Wq, L0_attn_bq, L0_attn_Wk,
           L0_attn_bk, L0_attn_Wv, L0_attn_bv, L0_attn_Wo, L0_attn_bo,
           L0_ffn_W_fc, L0_ffn_W_proj):
    global _NC_CACHE
    x = np.ascontiguousarray(np.asarray(L0_attn_hidden_states,
                                        dtype=np.float32))
    base = {
        "wq": np.ascontiguousarray(np.asarray(L0_attn_Wq, dtype=np.float32)),
        "wk": np.ascontiguousarray(np.asarray(L0_attn_Wk, dtype=np.float32)),
        "wv": np.ascontiguousarray(np.asarray(L0_attn_Wv, dtype=np.float32)),
        "wo": np.ascontiguousarray(np.asarray(L0_attn_Wo, dtype=np.float32)),
        "bq": np.ascontiguousarray(
            np.asarray(L0_attn_bq, dtype=np.float32).reshape(1, D)),
        "bk": np.ascontiguousarray(
            np.asarray(L0_attn_bk, dtype=np.float32).reshape(1, D)),
        "bv": np.ascontiguousarray(
            np.asarray(L0_attn_bv, dtype=np.float32).reshape(1, D)),
        "bo": np.ascontiguousarray(
            np.asarray(L0_attn_bo, dtype=np.float32).reshape(1, D)),
        "wfc": np.ascontiguousarray(
            np.asarray(L0_ffn_W_fc, dtype=np.float32).T),
        "wpr": np.ascontiguousarray(
            np.asarray(L0_ffn_W_proj, dtype=np.float32).T),
    }
    if _NC_CACHE is None:
        _NC_CACHE = build_nc()
    nc = _NC_CACHE
    in_maps = [dict(base, x=np.ascontiguousarray(x[i])) for i in range(8)]
    res = run_bass_kernel_spmd(nc, in_maps, core_ids=list(range(8)))
    out = np.stack([np.asarray(res.results[i]["out"]) for i in range(8)],
                   axis=0)
    return out
